# version 05 (final)
# baseline (speedup 1.0000x reference)
import numpy as np
from contextlib import ExitStack
import concourse.bass as bass
import concourse.mybir as mybir
from concourse.bass_utils import run_bass_kernel_spmd

F32 = mybir.dt.float32
BF16 = mybir.dt.bfloat16
AF = mybir.ActivationFunctionType
ALU = mybir.AluOpType
AX = mybir.AxisListType

NCORES = 8
SPC = 4
T = 2048
D = 1024
NCH = 8
NH = 16
DH = 64
NQT = 16
TG = 512
NTG = 4
DEPTH = 6
DI = 2730
NMC = 22
MCL = 42
INNER3 = 3072
EPS = 1e-5
ASCALE = float(1.0 / np.sqrt(2048.0))
T1 = T + 1

ENGS = ("t", "v", "s", "q", "g")


class Res:
    __slots__ = ("w", "r")

    def __init__(self):
        self.w = {}
        self.r = {}

    def rdeps(self, e):
        return [(k, v) for k, v in self.w.items() if k != e]

    def wdeps(self, e):
        d = dict(self.w)
        for k, v in self.r.items():
            d[k] = max(d.get(k, 0), v)
        return [(k, v) for k, v in d.items() if k != e]


PARANOID = True

class PB:
    def __init__(self, nc):
        self.nc = nc
        self.stream = {e: [] for e in ENGS}
        self.n = {e: 0 for e in ENGS}
        self._barrier = None

    def barrier(self):
        self._barrier = dict(self.n)
        self._bseen = set()

    def g(self, e, fns, reads=(), writes=(), extra=()):
        waits = {}
        if self._barrier is not None and e not in self._bseen:
            self._bseen.add(e)
            for k, v in self._barrier.items():
                if k != e and v > 0:
                    waits[k] = max(waits.get(k, 0), v)
            if len(self._bseen) == len(ENGS):
                self._barrier = None
        for r in reads:
            for (k, v) in r.rdeps(e):
                waits[k] = max(waits.get(k, 0), v)
        for r in writes:
            for (k, v) in r.wdeps(e):
                waits[k] = max(waits.get(k, 0), v)
        for (k, v) in extra:
            if k != e:
                waits[k] = max(waits.get(k, 0), v)
        for qk in ("q", "g"):
            if qk in waits:
                waits[qk] = self.n[qk]
        if PARANOID:
            for k in ENGS:
                if k != e and self.n[k] > 0:
                    waits[k] = self.n[k]
        if not isinstance(fns, (list, tuple)):
            fns = [fns]
        self.stream[e].append((tuple(waits.items()), tuple(fns), 1, e))
        self.n[e] += 1
        m = self.n[e]
        for r in reads:
            r.r[e] = m
        for r in writes:
            r.w = {e: m}
            r.r = {}
        return m

    def dma(self, e, out, in_, reads=(), writes=(), extra=()):
        waits = {}
        if self._barrier is not None and e not in self._bseen:
            self._bseen.add(e)
            for k, v in self._barrier.items():
                if k != e and v > 0:
                    waits[k] = max(waits.get(k, 0), v)
            if len(self._bseen) == len(ENGS):
                self._barrier = None
        for r in reads:
            for (k, v) in r.rdeps(e):
                waits[k] = max(waits.get(k, 0), v)
        for r in writes:
            for (k, v) in r.wdeps(e):
                waits[k] = max(waits.get(k, 0), v)
        for (k, v) in extra:
            if k != e:
                waits[k] = max(waits.get(k, 0), v)
        for qk in ("q", "g"):
            if qk in waits:
                waits[qk] = self.n[qk]
        if PARANOID:
            for k in ENGS:
                if k != e and self.n[k] > 0:
                    waits[k] = self.n[k]
        fn = lambda eng, o=out, i=in_: eng.dma_start(out=o, in_=i)
        self.stream[e].append((tuple(waits.items()), (fn,), 16, e))
        self.n[e] += 16
        m = self.n[e]
        for r in reads:
            r.r[e] = m
        for r in writes:
            r.w = {e: m}
            r.r = {}
        return m

    def finalize(self, sems):
        with self.nc.Block() as block:
            def mk(eng_name):
                def body(eng):
                    hiw = {}
                    for (waits, fns, inc, _e) in self.stream[eng_name]:
                        for (se, v) in waits:
                            if v > hiw.get(se, 0):
                                hiw[se] = v
                                eng.wait_ge(sems[se], v)
                        ins = None
                        for fn in fns:
                            ins = fn(eng)
                        ins.then_inc(sems[eng_name], inc)
                return body
            block.tensor(mk("t"))
            block.vector(mk("v"))
            block.scalar(mk("s"))
            block.sync(mk("q"))
            block.gpsimd(mk("g"))


def _host_consts():
    c = {}
    half = np.arange(0, DH, 2, dtype=np.float64)
    inv_freq = 1.0 / 10000.0 ** (half / DH)
    t = np.arange(T, dtype=np.float64)
    freqs = t[:, None] * inv_freq[None, :]
    base_scale = (half + 0.4 * DH) / (1.4 * DH)
    power = (t - T // 2) / 128.0
    scale = base_scale[None, :] ** power[:, None]
    idx = np.tile(np.arange(32), 4)
    c["cs"] = np.ascontiguousarray((np.cos(freqs) * scale).T[idx]).astype(np.float32)
    c["sn"] = np.ascontiguousarray((np.sin(freqs) * scale).T[idx]).astype(np.float32)
    c["ci"] = np.ascontiguousarray((np.cos(freqs) / scale).T[idx]).astype(np.float32)
    c["si"] = np.ascontiguousarray((np.sin(freqs) / scale).T[idx]).astype(np.float32)
    jj = np.arange(128)[:, None]
    ii = np.arange(128)[None, :]
    c["mask_own"] = (jj <= ii).astype(np.float32)
    c["mask_old"] = (jj >= ii).astype(np.float32)
    return c


def _qk_perm():
    perm = np.zeros(D, np.int64)
    for h in range(NH):
        for d in range(DH):
            new = (2 * (h // 4) + (d // 32)) * 128 + 32 * (h % 4) + (d % 32)
            perm[new] = h * DH + d
    return perm


def _host_prep(inputs):
    d = {}
    perm = _qk_perm()
    Wqkv = np.asarray(inputs["L0_ffn_Wqkv"], np.float32)
    d["wqkv"] = np.ascontiguousarray(np.concatenate(
        [Wqkv[:, :, 0:D][:, :, perm], Wqkv[:, :, D:2 * D][:, :, perm], Wqkv[:, :, 2 * D:]], axis=2))
    d["wo"] = np.ascontiguousarray(np.asarray(inputs["L0_ffn_Wo"], np.float32))
    d["w1"] = np.ascontiguousarray(np.asarray(inputs["L0_ffn_W1"], np.float32))
    d["w2"] = np.ascontiguousarray(np.asarray(inputs["L0_ffn_W2"], np.float32))
    d["wa"] = np.ascontiguousarray(np.asarray(inputs["L0_attn_W_qkv"], np.float32))

    b_qkv = np.asarray(inputs["L0_attn_b_qkv"], np.float32)
    d["ba_col"] = np.ascontiguousarray(b_qkv.reshape(24, 128).T)
    d["bav_bc"] = np.ascontiguousarray(np.broadcast_to(b_qkv[2 * D:][None, :], (128, D)))

    ln_g = np.asarray(inputs["L0_ffn_ln_g"], np.float32)
    ln_b = np.asarray(inputs["L0_ffn_ln_b"], np.float32)
    d["lng_col"] = np.ascontiguousarray(ln_g.reshape(DEPTH, NCH, 128).transpose(2, 0, 1))
    d["lnb_col"] = np.ascontiguousarray(ln_b.reshape(DEPTH, NCH, 128).transpose(2, 0, 1))
    b1 = np.asarray(inputs["L0_ffn_b1"], np.float32)
    b1a = np.zeros((DEPTH, NMC * 128), np.float32)
    b1g = np.zeros((DEPTH, NMC * 128), np.float32)
    b1a[:, :DI] = b1[:, :DI]
    b1g[:, :DI] = b1[:, DI:]
    d["b1a_col"] = np.ascontiguousarray(b1a.reshape(DEPTH, NMC, 128).transpose(2, 0, 1))
    d["b1g_col"] = np.ascontiguousarray(b1g.reshape(DEPTH, NMC, 128).transpose(2, 0, 1))
    d["b2_col"] = np.ascontiguousarray(
        np.asarray(inputs["L0_ffn_b2"], np.float32).reshape(DEPTH, NCH, 128).transpose(2, 0, 1))

    sel01 = np.zeros((4, 128), np.float32)
    for j in range(4):
        sel01[j, 32 * j:32 * j + 32] = 1.0
    d["sel01"] = sel01

    def mk_col(sc):
        col = np.zeros((128, 2, DEPTH), np.float32)
        for p_ in range(128):
            for hf in range(2):
                col[p_, hf, :] = sc[:, 32 * hf + (p_ % 32)]
        return np.ascontiguousarray(col)
    d["qs_col"] = mk_col(np.asarray(inputs["L0_ffn_q_scale"], np.float32))
    d["ks_col"] = mk_col(np.asarray(inputs["L0_ffn_k_scale"], np.float32))
    return d


_DEBUG = None


def build(debug=None):
    nc = bass.Bass("TRN2", target_bir_lowering=False)

    x_in = nc.dram_tensor("x", [SPC, NCH, 128, T], F32, kind="ExternalInput").ap()
    out_t = nc.dram_tensor("out", [SPC, NCH, 128, T], F32, kind="ExternalOutput").ap()
    wqkv = nc.dram_tensor("wqkv", [DEPTH, D, INNER3], F32, kind="ExternalInput").ap()
    wo = nc.dram_tensor("wo", [DEPTH, D, D], F32, kind="ExternalInput").ap()
    w1 = nc.dram_tensor("w1", [DEPTH, D, 2 * DI], F32, kind="ExternalInput").ap()
    w2 = nc.dram_tensor("w2", [DEPTH, DI, D], F32, kind="ExternalInput").ap()
    wa = nc.dram_tensor("wa", [D, INNER3], F32, kind="ExternalInput").ap()
    cns = {}
    for nm, shp in [("cs", [128, T]), ("sn", [128, T]), ("ci", [128, T]), ("si", [128, T]),
                    ("mask_own", [128, 128]), ("mask_old", [128, 128]),
                    ("ba_col", [128, 24]), ("bav_bc", [128, D]),
                    ("lng_col", [128, DEPTH, NCH]), ("lnb_col", [128, DEPTH, NCH]),
                    ("b1a_col", [128, DEPTH, NMC]), ("b1g_col", [128, DEPTH, NMC]),
                    ("b2_col", [128, DEPTH, NCH]),
                    ("sel01", [4, 128]), ("qs_col", [128, 2, DEPTH]), ("ks_col", [128, 2, DEPTH])]:
        cns[nm] = nc.dram_tensor(nm, shp, F32, kind="ExternalInput").ap()

    wqkv_bf = nc.dram_tensor("wqkv_bf", [DEPTH, D, INNER3], BF16, kind="Internal").ap()
    wo_bf = nc.dram_tensor("wo_bf", [DEPTH, D, D], BF16, kind="Internal").ap()
    w1_bf = nc.dram_tensor("w1_bf", [DEPTH, D, 2 * DI], BF16, kind="Internal").ap()
    w2_bf = nc.dram_tensor("w2_bf", [DEPTH, DI, D], BF16, kind="Internal").ap()
    wa_bf = nc.dram_tensor("wa_bf", [D, INNER3], BF16, kind="Internal").ap()
    Ya = nc.dram_tensor("Ya", [SPC, NCH, 128, T], F32, kind="Internal").ap()
    Yb = nc.dram_tensor("Yb", [SPC, NCH, 128, T], F32, kind="Internal").ap()
    Y1 = nc.dram_tensor("Y1", [SPC, NCH, 128, T], F32, kind="Internal").ap()

    AT = nc.alloc_sbuf_tensor_at
    cur = [(nc.sbuf_base + 31) & ~31]

    def at(name, shape, dtype, region_off=None):
        o = cur[0] if region_off is None else region_off
        h = AT(name, shape, dtype, offset=o)
        if region_off is None:
            elems = 1
            for s_ in shape[1:]:
                elems *= s_
            cur[0] = (o + elems * (2 if dtype == BF16 else 4) + 31) & ~31
        return h.ap()

    ba_col = at("ba_col_sb", [128, 24], F32)
    bav = at("bav_sb", [128, D], F32)
    lng = at("lng_sb", [128, DEPTH * NCH], F32)
    lnb = at("lnb_sb", [128, DEPTH * NCH], F32)
    b1a = at("b1a_sb", [128, DEPTH * NMC], F32)
    b1g = at("b1g_sb", [128, DEPTH * NMC], F32)
    b2c = at("b2_sb", [128, DEPTH * NCH], F32)
    sel01 = at("sel01_sb", [4, 128], F32)
    qscol = at("qscol_sb", [128, 2 * DEPTH], F32)
    kscol = at("kscol_sb", [128, 2 * DEPTH], F32)
    mown = at("mown", [128, 128], BF16)
    mold = at("mold", [128, 128], BF16)
    ones_bf = at("ones_bf", [128, 8], BF16)
    ones1f = at("ones1f", [1, 128], F32)
    bo32 = at("bo32", [128, 8], BF16)

    tab_off = cur[0]
    cs = at("cs_sb", [128, T], F32)
    sn = at("sn_sb", [128, T], F32)
    ci = at("ci_sb", [128, T], F32)
    si = at("si_sb", [128, T], F32)
    xf32 = at("xf32", [128, NCH * TG], F32, tab_off)
    y1sb = at("y1sb", [128, NCH * TG], F32, tab_off + NCH * TG * 4)

    ybf_off = cur[0]
    ybf = at("ybf", [128, NCH * T1], BF16)
    kT_A = at("kT_A", [128, NCH * T], BF16, ybf_off)
    yf = at("yf", [128, 2 * NCH * 513], F32, ybf_off)
    cur[0] = max(cur[0], ybf_off + 2 * NCH * 513 * 4 + 32)

    wr_off = cur[0]
    wring = at("wring", [128, 10240], BF16)

    act_off = cur[0]

    def view(specs):
        o = act_off
        outm = {}
        for nm, shape, dt_ in specs:
            h = AT(nm, shape, dt_, offset=o)
            elems = 1
            for s_ in shape[1:]:
                elems *= s_
            o = (o + elems * (2 if dt_ == BF16 else 4) + 31) & ~31
            outm[nm] = h.ap()
        return outm, o

    mv, END_ATTN = view([
        ("qTA", [128, T], BF16), ("qTB", [128, T], BF16),
        ("kTA", [128, T], BF16), ("kTB", [128, T], BF16),
        ("vaug", [128, NQT * 260], BF16),
        ("att", [128, NCH * T], BF16),
        ("ptile", [128, 4 * 4 * 384], BF16),
        ("rotf", [128, 4 * TG], F32),
        ("srows4", [4, 1024], F32),
        ("asums", [1, 2048], F32),
        ("sqbf", [128, 2 * TG], BF16),
        ("yres", [128, 2 * TG], F32),
        ("fscr", [128, 2 * TG], F32),
    ])
    qTA, qTB, kTA, kTB, vaug, att, ptile, rotf, srows4, asums, sqbf, yres, fscr = (
        mv["qTA"], mv["qTB"], mv["kTA"], mv["kTB"], mv["vaug"], mv["att"],
        mv["ptile"], mv["rotf"], mv["srows4"], mv["asums"], mv["sqbf"], mv["yres"], mv["fscr"])

    fv, END_FFN = view([
        ("xnbf", [128, 2 * NCH * TG], BF16),
        ("mT", [128, NMC * TG], BF16),
        ("gelu_t", [128, 2 * TG], F32),
        ("f_srows", [1, 2048], F32),
        ("f_sqbf", [128, 2 * TG], BF16),
        ("f_fscr", [128, 3 * TG], F32),
        ("w2res", [128, NMC * 1024], BF16),
    ])
    xnbf, mT, gelu_t, f_srows, f_sqbf, f_fscr, w2res = (
        fv["xnbf"], fv["mT"], fv["gelu_t"], fv["f_srows"], fv["f_sqbf"],
        fv["f_fscr"], fv["w2res"])

    av, END_A = view([
        ("vaugA", [128, NQT * 1025], BF16),
        ("qA_sb", [128, NCH * TG], BF16),
        ("hbf", [128, NCH * TG], BF16),
        ("a_ptile", [128, 2 * 256], BF16),
        ("a_rotf", [128, 4 * TG], F32),
        ("a_srows", [1, 1024], F32),
        ("a_sq", [128, 2 * TG], BF16),
        ("a_xbf", [128, NCH * TG], BF16),
        ("sbrow", [128, 256], F32),
        ("a_prod", [128, NCH * 256], F32),
    ])
    vaugA, qA_sb, hbf, a_ptile, a_rotf, a_srows, a_sq, a_xbf, sbrow, a_prod = (
        av["vaugA"], av["qA_sb"], av["hbf"], av["a_ptile"], av["a_rotf"],
        av["a_srows"], av["a_sq"], av["a_xbf"], av["sbrow"], av["a_prod"])

    pv_, END_PC = view([
        ("pc32", [128, 2 * 5460], F32),
        ("pcbf", [128, 2 * 5460], BF16),
    ])
    pc32, pcbf = pv_["pc32"], pv_["pcbf"]

    sbuf_used = max(END_ATTN, END_FFN, END_A, END_PC)
    assert sbuf_used <= nc.sbuf_top, f"sbuf map over budget: {sbuf_used} > {nc.sbuf_top}"

    pA = nc.alloc_psum_tensor("pA", [128, 2048], F32).ap()
    pB = nc.alloc_psum_tensor("pB", [128, 2048], F32).ap()

    p = PB(nc)
    R = {}

    def res(name):
        if name not in R:
            R[name] = Res()
        return R[name]

    pa = [res(f"pa{b}") for b in range(4)]
    pb = [res(f"pb{b}") for b in range(4)]

    def paS(b, lo=0, n=512):
        return pA[:, b * 512 + lo: b * 512 + lo + n]

    def pbS(b, lo=0, n=512):
        return pB[:, b * 512 + lo: b * 512 + lo + n]

    for nm, dst in [("ba_col", ba_col), ("bav_bc", bav), ("lng_col", lng),
                    ("lnb_col", lnb), ("b1a_col", b1a), ("b1g_col", b1g),
                    ("b2_col", b2c)]:
        src = cns[nm]
        if src.ndim == 3:
            src = src.rearrange("p a b -> p (a b)")
        p.dma("q", dst, src, writes=[res("c_" + nm)])
    p.dma("q", sel01, cns["sel01"], writes=[res("c_selq")])
    p.dma("q", qscol, cns["qs_col"].rearrange("p h l -> p (h l)"), writes=[res("c_selq")])
    p.dma("q", kscol, cns["ks_col"].rearrange("p h l -> p (h l)"), writes=[res("c_selq")])
    rms = res("mstage")
    p.dma("q", pc32[:, 0:128], cns["mask_own"], writes=[rms])
    p.dma("q", pc32[:, 128:256], cns["mask_old"], writes=[rms])
    p.g("v", [lambda eng: eng.tensor_copy(out=mown, in_=pc32[:, 0:128]),
              lambda eng: eng.tensor_copy(out=mold, in_=pc32[:, 128:256])],
        reads=[rms], writes=[res("c_masks")])
    res("pcs0").r["v"] = p.n["v"]
    res("pcs1").r["v"] = p.n["v"]
    p.g("v", [lambda eng: eng.memset(ones_bf[:, 0:1], 1.0),
              lambda eng: eng.memset(ones1f, 1.0),
              lambda eng: eng.memset(bo32[:, 0:4], 0.0)] +
        [lambda eng, j=j: eng.memset(bo32[32 * j:32 * j + 32, j:j + 1], 1.0)
         for j in range(4)], writes=[res("c_ones")])

    pc_jobs = []
    for kc in range(NCH):
        pc_jobs.append(("wa", wa[128 * kc:128 * kc + 128, :], wa_bf[128 * kc:128 * kc + 128, :], 128, INNER3))
    for L in range(DEPTH):
        for kc in range(NCH):
            pc_jobs.append((f"wqkv{L}", wqkv[L, 128 * kc:128 * kc + 128, :],
                            wqkv_bf[L, 128 * kc:128 * kc + 128, :], 128, INNER3))
        for kc in range(NCH):
            pc_jobs.append((f"wo{L}", wo[L, 128 * kc:128 * kc + 128, :],
                            wo_bf[L, 128 * kc:128 * kc + 128, :], 128, D))
        for kc in range(NCH):
            pc_jobs.append((f"w1{L}", w1[L, 128 * kc:128 * kc + 128, :],
                            w1_bf[L, 128 * kc:128 * kc + 128, :], 128, 2 * DI))
        for mc in range(NMC):
            r_ = 128 if mc < NMC - 1 else MCL
            pc_jobs.append((f"w2{L}", w2[L, 128 * mc:128 * mc + r_, :],
                            w2_bf[L, 128 * mc:128 * mc + r_, :], r_, D))
    pc_done = {}
    for ji, (fam, src, dst, rows, cols) in enumerate(pc_jobs):
        sl = ji % 2
        stg = pc32[:, sl * 5460: sl * 5460 + cols]
        stb = pcbf[:, sl * 5460: sl * 5460 + cols]
        rs, rb = res(f"pcs{sl}"), res(f"pcb{sl}")
        p.dma("q", stg[0:rows, :], src, writes=[rs])
        p.g("g", lambda eng, o=stb, i=stg, r_=rows: eng.tensor_copy(out=o[0:r_, :], in_=i[0:r_, :]),
            reads=[rs], writes=[rb])
        p.dma("g", dst, stb[0:rows, :], reads=[rb])
        pc_done[fam] = p.n["g"]
    precast_gate = [("g", p.n["g"]), ("q", p.n["q"])]

    rk = res("a_k")
    rq = res("a_q")
    rv = res("a_v")
    rh = res("a_hbf")

    def stage_a_ln(src_sb, dst_sb, rsrc, rdst, tmp32=None):
        rxbf, rsq = res("a_xbf_r"), res("a_sq_r")
        p.g("v", [lambda eng, ch=ch: eng.tensor_copy(
            out=a_xbf[:, ch * TG:(ch + 1) * TG], in_=src_sb[:, ch * TG:(ch + 1) * TG])
            for ch in range(NCH)], reads=[rsrc], writes=[rxbf])
        p.g("t", [lambda eng, ch=ch: eng.matmul(
            pA[0:1, 0:TG], ones_bf[:, 0:1], a_xbf[:, ch * TG:(ch + 1) * TG],
            start=(ch == 0), stop=(ch == NCH - 1)) for ch in range(NCH)],
            reads=[rxbf, res("c_ones")], writes=[pa[0]])
        for ch in range(NCH):
            sl = ch % 2
            rs = res(f"a_sqs{sl}")
            p.g("v", [lambda eng, ch=ch, sl=sl: eng.tensor_tensor(
                out=a_sq[:, sl * TG:sl * TG + TG],
                in0=a_xbf[:, ch * TG:(ch + 1) * TG], in1=a_xbf[:, ch * TG:(ch + 1) * TG],
                op=ALU.mult)], reads=[rxbf], writes=[rs])
            p.g("t", [lambda eng, ch=ch, sl=sl: eng.matmul(
                pA[0:1, 512:512 + TG], ones_bf[:, 0:1], a_sq[:, sl * TG:sl * TG + TG],
                start=(ch == 0), stop=(ch == NCH - 1))], reads=[rs], writes=[pa[1]])
        rrow = res("a_rows")
        p.g("v", [
            lambda eng: eng.tensor_scalar_mul(out=a_srows[0:1, 0:TG], in0=pA[0:1, 0:TG], scalar1=1.0 / D),
            lambda eng: eng.tensor_scalar_mul(out=a_srows[0:1, 512:1024], in0=pA[0:1, 512:512 + TG], scalar1=1.0 / D),
            lambda eng: eng.tensor_tensor(out=a_rotf[0:1, 0:TG], in0=a_srows[0:1, 0:TG],
                                          in1=a_srows[0:1, 0:TG], op=ALU.mult),
            lambda eng: eng.tensor_tensor(out=a_srows[0:1, 512:1024], in0=a_srows[0:1, 512:1024],
                                          in1=a_rotf[0:1, 0:TG], op=ALU.subtract),
            lambda eng: eng.tensor_scalar_add(out=a_srows[0:1, 512:1024], in0=a_srows[0:1, 512:1024],
                                              scalar1=EPS),
        ], reads=[pa[0], pa[1]], writes=[rrow, res("a_rotf_row")])
        p.g("s", [lambda eng: eng.activation(out=a_srows[0:1, 512:1024], in_=a_srows[0:1, 512:1024],
                                             func=AF.Sqrt)], reads=[rrow], writes=[rrow])
        p.g("v", [lambda eng: eng.reciprocal(out=a_srows[0:1, 512:1024], in_=a_srows[0:1, 512:1024])],
            reads=[rrow], writes=[rrow])
        p.g("t", [
            lambda eng: eng.matmul(paS(2), ones1f, a_srows[0:1, 0:TG], start=True, stop=True),
            lambda eng: eng.matmul(paS(3), ones1f, a_srows[0:1, 512:1024], start=True, stop=True),
        ], reads=[rrow, res("c_ones")], writes=[pa[2], pa[3]])
        ops = []
        for ch in range(NCH):
            sl = ch % 2
            ops.append(lambda eng, ch=ch, sl=sl: eng.tensor_tensor(
                out=a_rotf[:, sl * TG:sl * TG + TG],
                in0=src_sb[:, ch * TG:(ch + 1) * TG], in1=paS(2), op=ALU.subtract))
            ops.append(lambda eng, ch=ch, sl=sl: eng.tensor_tensor(
                out=dst_sb[:, ch * TG:(ch + 1) * TG],
                in0=a_rotf[:, sl * TG:sl * TG + TG], in1=paS(3), op=ALU.mult))
        p.g("v", ops, reads=[rsrc, pa[2], pa[3]], writes=[rdst, res("a_rotf_big")])

    def stage_a_seq(s):
        for tg in range(NTG):
            lo = tg * TG
            rx = res("a_xf32")
            for ch in range(NCH):
                p.dma("q", xf32[:, ch * TG:(ch + 1) * TG], x_in[s, ch, :, lo:lo + TG],
                      writes=[rx], extra=precast_gate if (s == 0 and tg == 0) else ())
            stage_a_ln(xf32, hbf, rx, rh)

            for oc in range(24):
                wsl = res(f"a_wsl{oc % 4}")
                slab = wring[:, (oc % 4) * 1024:(oc % 4) * 1024 + 1024]
                p.dma("q", slab.rearrange("p (kc c) -> p kc c", c=128),
                      wa_bf[:, oc * 128:(oc + 1) * 128].rearrange("(kc p) c -> p kc c", p=128),
                      writes=[wsl], extra=[("g", pc_done["wa"])])
                if oc < 16:
                    bank = oc % 2
                    p.g("t", [lambda eng, kc=kc, oc=oc, bank=bank: eng.matmul(
                        pbS(bank), wring[:, (oc % 4) * 1024 + kc * 128:(oc % 4) * 1024 + kc * 128 + 128],
                        hbf[:, kc * TG:(kc + 1) * TG], start=(kc == 0), stop=(kc == NCH - 1))
                        for kc in range(NCH)], reads=[wsl, rh], writes=[pb[bank]])
                    if oc < 8:
                        dst = qA_sb[:, oc * TG:oc * TG + TG]
                        rdst = rq
                    else:
                        dst = kT_A[:, (oc - 8) * T + lo:(oc - 8) * T + lo + TG]
                        rdst = rk
                    p.g("s", [lambda eng, dst=dst, bank=bank, oc=oc: eng.activation(
                        out=dst, in_=pbS(bank), func=AF.Identity, bias=ba_col[:, oc:oc + 1])],
                        reads=[pb[bank], res("c_ba_col")], writes=[rdst])
                else:
                    fc = oc - 16
                    for tt4 in range(4):
                        tile = tg * 4 + tt4
                        bank = 2 + (tt4 % 2)
                        p.g("t", [lambda eng, kc=kc, oc=oc, tt4=tt4, bank=bank: eng.matmul(
                            pbS(bank, 0, 128),
                            hbf[:, kc * TG + tt4 * 128: kc * TG + tt4 * 128 + 128],
                            wring[:, (oc % 4) * 1024 + kc * 128:(oc % 4) * 1024 + kc * 128 + 128],
                            start=(kc == 0), stop=(kc == NCH - 1)) for kc in range(NCH)],
                            reads=[wsl, rh], writes=[pb[bank]])
                        p.g("v", [lambda eng, tile=tile, fc=fc, bank=bank: eng.tensor_tensor(
                            out=vaugA[:, tile * 1025 + fc * 128: tile * 1025 + fc * 128 + 128],
                            in0=pbS(bank, 0, 128), in1=bav[:, fc * 128:fc * 128 + 128],
                            op=ALU.add)], reads=[pb[bank], res("c_bav_bc")], writes=[rv])

            if debug in ("q", "k", "v"):
                rdump = res("a_y1")
                if debug == "q":
                    p.g("v", [lambda eng, ch=ch: eng.tensor_copy(
                        out=y1sb[:, ch * TG:(ch + 1) * TG], in_=qA_sb[:, ch * TG:(ch + 1) * TG])
                        for ch in range(NCH)], reads=[rq], writes=[rdump])
                elif debug == "k":
                    p.g("v", [lambda eng, ch=ch, lo=lo: eng.tensor_copy(
                        out=y1sb[:, ch * TG:(ch + 1) * TG], in_=kT_A[:, ch * T + lo:ch * T + lo + TG])
                        for ch in range(NCH)], reads=[rk], writes=[rdump])
                else:
                    p.g("v", [lambda eng, tt4=tt4, tg=tg: eng.tensor_copy(
                        out=y1sb[:, tt4 * 1024:(tt4 + 1) * 1024],
                        in_=vaugA[:, (tg * 4 + tt4) * 1025:(tg * 4 + tt4) * 1025 + 1024])
                        for tt4 in range(4)], reads=[rv], writes=[rdump])
                for ch in range(NCH):
                    p.dma("g", out_t[s, ch, :, lo:lo + TG], y1sb[:, ch * TG:(ch + 1) * TG], reads=[rdump])

            for pr in range(2):
                qp = tg * 2 + pr
                nk = 2 * qp + 2
                for k in range(nk):
                    diag2 = (k == 2 * qp + 1)
                    ncols = 128 if diag2 else 256
                    qoff = 128 if diag2 else 0
                    bank = 2 + (k % 2)
                    rpt = res(f"a_pt{k % 2}")
                    p.g("t", [lambda eng, ch=ch, k=k, bank=bank, pr=pr, qoff=qoff, ncols=ncols: eng.matmul(
                        pbS(bank, 0, ncols),
                        kT_A[:, ch * T + k * 128: ch * T + k * 128 + 128],
                        qA_sb[:, ch * TG + pr * 256 + qoff: ch * TG + pr * 256 + qoff + ncols],
                        start=(ch == 0), stop=(ch == NCH - 1)) for ch in range(NCH)],
                        reads=[rk, rq], writes=[pb[bank]])
                    p.g("s", [lambda eng, k=k, bank=bank, ncols=ncols: eng.activation(
                        out=a_ptile[:, (k % 2) * 256:(k % 2) * 256 + ncols],
                        in_=pbS(bank, 0, ncols), func=AF.Exp, scale=ASCALE)],
                        reads=[pb[bank]], writes=[rpt])
                    if k >= 2 * qp:
                        p.g("v", [lambda eng, k=k: eng.tensor_tensor(
                            out=a_ptile[:, (k % 2) * 256:(k % 2) * 256 + 128],
                            in0=a_ptile[:, (k % 2) * 256:(k % 2) * 256 + 128],
                            in1=mown, op=ALU.mult)],
                            reads=[res("c_masks")], writes=[rpt])
                    mms = []
                    for oc in range(NCH):
                        mms.append(lambda eng, oc=oc, k=k, qoff=qoff, ncols=ncols, nk=nk: eng.matmul(
                            pA[:, oc * 256 + qoff: oc * 256 + qoff + ncols],
                            vaugA[:, k * 1025 + oc * 128: k * 1025 + oc * 128 + 128],
                            a_ptile[:, (k % 2) * 256:(k % 2) * 256 + ncols],
                            start=(k == 0 and oc % 2 == 0), stop=(k == nk - 1)))
                    mms.append(lambda eng, k=k, qoff=qoff, ncols=ncols, nk=nk: eng.matmul(
                        pB[0:1, 512 + 256 + qoff: 512 + 256 + qoff + ncols],
                        vaugA[:, k * 1025 + 1024: k * 1025 + 1025],
                        a_ptile[:, (k % 2) * 256:(k % 2) * 256 + ncols],
                        start=(k == 0), stop=(k == nk - 1)))
                    p.g("t", mms, reads=[rpt, rv], writes=[pa[0], pa[1], pa[2], pa[3], pb[1]])
                if debug == "pv0" and s == 0 and tg == 0 and pr == 0:
                    rdpv = res("a_dpv")
                    p.g("v", [lambda eng: eng.tensor_copy(out=a_rotf[:, 0:2048], in_=pA[:, 0:2048])],
                        reads=[pa[0], pa[1], pa[2], pa[3]], writes=[rdpv])
                    p.dma("g", out_t[0, 0, :, 0:2048], a_rotf[:, 0:2048], reads=[rdpv])
                    p.g("v", [lambda eng: eng.tensor_copy(out=sbrow[0:1, 0:256], in_=pB[0:1, 768:1024])],
                        reads=[pb[1]], writes=[rdpv])
                    p.dma("g", out_t[0, 1, 0:1, 0:256], sbrow[0:1, 0:256], reads=[rdpv])
                rrow2 = res("a_row2")
                p.g("v", [lambda eng: eng.reciprocal(out=a_srows[0:1, 0:256], in_=pB[0:1, 768:1024])],
                    reads=[pb[1]], writes=[rrow2])
                p.g("t", [lambda eng: eng.matmul(pbS(2, 0, 256), ones1f, a_srows[0:1, 0:256],
                                                 start=True, stop=True)],
                    reads=[rrow2, res("c_ones")], writes=[pb[2]])
                p.g("v", [lambda eng: eng.tensor_copy(out=sbrow, in_=pbS(2, 0, 256))],
                    reads=[pb[2]], writes=[res("a_sbrow")])
                p.g("v", [lambda eng: eng.tensor_tensor(
                    out=a_prod[:].rearrange("p (c n) -> p c n", c=NCH),
                    in0=pA[:].rearrange("p (c n) -> p c n", c=NCH),
                    in1=sbrow.rearrange("p (u n) -> p u n", u=1).broadcast_to([128, NCH, 256]),
                    op=ALU.mult)],
                    reads=[pa[0], pa[1], pa[2], pa[3], res("a_sbrow")], writes=[res("a_prodr")])
                p.g("v", [lambda eng, pr=pr: eng.tensor_tensor(
                    out=y1sb[:].rearrange("p (c n) -> p c n", c=NCH)[:, :, pr * 256:pr * 256 + 256],
                    in0=a_prod[:].rearrange("p (c n) -> p c n", c=NCH),
                    in1=xf32[:].rearrange("p (c n) -> p c n", c=NCH)[:, :, pr * 256:pr * 256 + 256],
                    op=ALU.add)],
                    reads=[res("a_prodr"), rx], writes=[res("a_y1")])

            ry1 = res("a_y1")
            for ch in range(NCH):
                p.dma("g", Y1[s, ch, :, lo:lo + TG], y1sb[:, ch * TG:(ch + 1) * TG], reads=[ry1])
            if debug == "x1":
                for ch in range(NCH):
                    p.dma("g", out_t[s, ch, :, lo:lo + TG], y1sb[:, ch * TG:(ch + 1) * TG], reads=[ry1])
            stage_a_ln(y1sb, xf32, ry1, rx)
            for ch in range(NCH):
                p.dma("g", Ya[s, ch, :, lo:lo + TG], xf32[:, ch * TG:(ch + 1) * TG], reads=[rx])
                if debug == "stageA":
                    p.dma("g", out_t[s, ch, :, lo:lo + TG], xf32[:, ch * TG:(ch + 1) * TG], reads=[rx])

    for s in range(SPC):
        if s == 0:
            p.g("v", [lambda eng, tt=tt: eng.memset(
                vaugA[:, tt * 1025 + 1024: tt * 1025 + 1025], 1.0) for tt in range(NQT)],
                writes=[res("a_v")], extra=precast_gate)
        stage_a_seq(s)

    for nm, dst in [("cs", cs), ("sn", sn), ("ci", ci), ("si", si)]:
        p.dma("q", dst, cns[nm], writes=[res("c_" + nm), res("c_tabs"), res("a_xf32"), res("a_y1")])

    ry_bf = res("m_ybf")
    rqT = res("m_qT")
    rkT = res("m_kT")
    rvaug = res("m_vaug")
    ratt = res("m_att")
    rw2res = res("m_w2res")

    def yf_sh(ysl, ch, n=TG):
        base = (ysl * NCH + ch) * 513 + (1 if ch < 4 else 0)
        return yf[:, base: base + n]

    def yf_plain(ysl, ch):
        b_ = (ysl * NCH + ch) * 513 + 1
        return yf[:, b_: b_ + TG]

    def ybf_in(ch, lo, n):
        base = ch * T1 + lo + (1 if ch < 4 else 0)
        return ybf[:, base: base + n]

    def attn_phase(L, s, Yin, Yout, part=4):
        p.barrier()
        if s == 0:
            p.g("v", [lambda eng, ch=ch: eng.memset(ybf[:, ch * T1:ch * T1 + 1], 0.0)
                      for ch in range(NCH)], writes=[ry_bf])
            p.g("v", [lambda eng, tt=tt: eng.memset(
                vaug[:, tt * 260:tt * 260 + 260].rearrange("p (h c) -> p h c", h=4)[:, :, 64:65],
                1.0) for tt in range(NQT)],
                writes=[rvaug])
        for ch in range(NCH):
            for tg in range(NTG):
                sl = (ch * NTG + tg) % 2
                ry = res(f"m_yld{sl}")
                p.dma("q", yres[:, sl * TG:(sl + 1) * TG], Yin[s, ch, :, tg * TG:(tg + 1) * TG],
                      writes=[ry])
                p.g("v", [lambda eng, ch=ch, tg=tg, sl=sl: eng.tensor_copy(
                    out=ybf[:, ch * T1 + tg * TG + 1: ch * T1 + tg * TG + 1 + TG],
                    in_=yres[:, sl * TG:(sl + 1) * TG])], reads=[ry], writes=[ry_bf])
        if part < 2:
            return

        for quad in range(4):
            sl_q = wring[:, 0:2048]
            sl_k = wring[:, 2048:4096]
            sl_v = wring[:, 4096:6144]
            wq_r, wk_r, wv_r = res("mw_q"), res("mw_k"), res("mw_v")
            gate = [("g", pc_done[f"wqkv{L}"])]
            p.dma("q", sl_q.rearrange("p (kc c) -> p kc c", c=256),
                  wqkv_bf[L, :, 256 * quad:256 * quad + 256].rearrange("(kc p) c -> p kc c", p=128),
                  writes=[wq_r], extra=gate)
            p.dma("q", sl_k.rearrange("p (kc c) -> p kc c", c=256),
                  wqkv_bf[L, :, 1024 + 256 * quad:1024 + 256 * quad + 256].rearrange("(kc p) c -> p kc c", p=128),
                  writes=[wk_r], extra=gate)
            p.dma("q", sl_v.rearrange("p (kc c) -> p kc c", c=256),
                  wqkv_bf[L, :, 2048 + 256 * quad:2048 + 256 * quad + 256].rearrange("(kc p) c -> p kc c", p=128),
                  writes=[wv_r], extra=gate)

            for tg in range(NTG):
                lo = tg * TG
                for (wsl, wr, b0) in ((sl_q, wq_r, 0), (sl_k, wk_r, 2)):
                    mms = []
                    for hf in range(2):
                        for kc in range(NCH):
                            mms.append(lambda eng, kc=kc, hf=hf, b0=b0, wsl=wsl, lo=lo: eng.matmul(
                                pbS(b0 + hf), wsl[:, kc * 256 + hf * 128: kc * 256 + hf * 128 + 128],
                                ybf_in(kc, lo, TG), start=(kc == 0), stop=(kc == NCH - 1)))
                    p.g("t", mms, reads=[wr, ry_bf], writes=[pb[b0], pb[b0 + 1]])
                for qk, b0, sccol, tabc, tabs, dst_r, dA, dB in (
                        ("q", 0, qscol, cs, sn, rqT, qTA, qTB),
                        ("k", 2, kscol, ci, si, rkT, kTA, kTB)):
                    rs0, rs1 = res("m_sq0"), res("m_sq1")
                    p.g("s", [lambda eng, b0=b0: eng.activation(
                        out=sqbf[:, 0:TG], in_=pbS(b0), func=AF.Square)],
                        reads=[pb[b0]], writes=[rs0])
                    p.g("t", [lambda eng: eng.matmul(
                        pA[0:4, 1024:1024 + TG], bo32[:, 0:4], sqbf[:, 0:TG],
                        start=True, stop=False)], reads=[rs0, res("c_ones")], writes=[pa[2]])
                    p.g("s", [lambda eng, b0=b0: eng.activation(
                        out=sqbf[:, TG:2 * TG], in_=pbS(b0 + 1), func=AF.Square)],
                        reads=[pb[b0 + 1]], writes=[rs1])
                    p.g("t", [lambda eng: eng.matmul(
                        pA[0:4, 1024:1024 + TG], bo32[:, 0:4], sqbf[:, TG:2 * TG],
                        start=False, stop=True)], reads=[rs1], writes=[pa[2]])
                    rr = res("m_l2rows")
                    p.g("s", [lambda eng: eng.activation(
                        out=srows4[0:4, 0:TG], in_=pA[0:4, 1024:1024 + TG], func=AF.Sqrt)],
                        reads=[pa[2]], writes=[rr])
                    p.g("v", [lambda eng: eng.tensor_scalar_max(
                        out=srows4[0:4, 0:TG], in0=srows4[0:4, 0:TG], scalar1=1e-12),
                        lambda eng: eng.reciprocal(out=srows4[0:4, 0:TG], in_=srows4[0:4, 0:TG])],
                        reads=[rr], writes=[rr])
                    p.g("t", [lambda eng: eng.matmul(
                        paS(2), sel01[0:4, :], srows4[0:4, 0:TG], start=True, stop=True)],
                        reads=[rr, res("c_selq")], writes=[pa[2]])
                    p.g("v", [
                        lambda eng, b0=b0, sccol=sccol, L=L: eng.tensor_scalar_mul(
                            out=rotf[:, 0:TG], in0=pbS(b0),
                            scalar1=sccol[:, 0 * DEPTH + L:0 * DEPTH + L + 1]),
                        lambda eng, b0=b0, sccol=sccol, L=L: eng.tensor_scalar_mul(
                            out=rotf[:, TG:2 * TG], in0=pbS(b0 + 1),
                            scalar1=sccol[:, 1 * DEPTH + L:1 * DEPTH + L + 1]),
                    ], reads=[pb[b0], pb[b0 + 1], res("c_selq")], writes=[res("m_rotf")])
                    if debug == "mqraw":
                        p.g("v", [lambda eng, dA=dA, lo=lo, b0=b0: eng.tensor_copy(
                            out=dA[:, lo:lo + TG], in_=pbS(b0)),
                            lambda eng, dB=dB, lo=lo, b0=b0: eng.tensor_copy(
                            out=dB[:, lo:lo + TG], in_=pbS(b0 + 1))],
                            reads=[pb[b0], pb[b0 + 1]], writes=[dst_r])
                    p.g("v", [
                        lambda eng, tabc=tabc, lo=lo: eng.tensor_tensor(
                            out=rotf[:, 2 * TG:3 * TG], in0=rotf[:, 0:TG],
                            in1=tabc[:, lo:lo + TG], op=ALU.mult),
                        lambda eng, tabs=tabs, lo=lo: eng.tensor_tensor(
                            out=rotf[:, 3 * TG:4 * TG], in0=rotf[:, TG:2 * TG],
                            in1=tabs[:, lo:lo + TG], op=ALU.mult),
                        lambda eng: eng.tensor_tensor(
                            out=rotf[:, 2 * TG:3 * TG], in0=rotf[:, 2 * TG:3 * TG],
                            in1=rotf[:, 3 * TG:4 * TG], op=ALU.subtract),
                    ], reads=[res("m_rotf"), res("c_tabs")], writes=[res("m_rotf")])
                    p.g("v", [lambda eng, dA=dA, lo=lo: eng.tensor_tensor(
                        out=dA[:, lo:lo + TG], in0=rotf[:, 2 * TG:3 * TG], in1=paS(2), op=ALU.mult)],
                        reads=[res("m_rotf"), pa[2]], writes=[dst_r])
                    p.g("v", [
                        lambda eng, tabc=tabc, lo=lo: eng.tensor_tensor(
                            out=rotf[:, 2 * TG:3 * TG], in0=rotf[:, TG:2 * TG],
                            in1=tabc[:, lo:lo + TG], op=ALU.mult),
                        lambda eng, tabs=tabs, lo=lo: eng.tensor_tensor(
                            out=rotf[:, 3 * TG:4 * TG], in0=rotf[:, 0:TG],
                            in1=tabs[:, lo:lo + TG], op=ALU.mult),
                        lambda eng: eng.tensor_tensor(
                            out=rotf[:, 2 * TG:3 * TG], in0=rotf[:, 2 * TG:3 * TG],
                            in1=rotf[:, 3 * TG:4 * TG], op=ALU.add),
                    ], reads=[res("m_rotf"), res("c_tabs")], writes=[res("m_rotf")])
                    p.g("v", [lambda eng, dB=dB, lo=lo: eng.tensor_tensor(
                        out=dB[:, lo:lo + TG], in0=rotf[:, 2 * TG:3 * TG], in1=paS(2), op=ALU.mult)],
                        reads=[res("m_rotf"), pa[2]], writes=[dst_r])
                for tt4 in range(4):
                    tile = tg * 4 + tt4
                    bank = tt4 % 2
                    p.g("t", [lambda eng, kc=kc, tt4=tt4, bank=bank, lo=lo: eng.matmul(
                        pA[:, bank * 512:bank * 512 + 256],
                        ybf_in(kc, lo + tt4 * 128, 128), sl_v[:, kc * 256:(kc + 1) * 256],
                        start=(kc == 0), stop=(kc == NCH - 1)) for kc in range(NCH)],
                        reads=[wv_r, ry_bf], writes=[pa[bank]])
                    p.g("s", [lambda eng, tile=tile, bank=bank: eng.activation(
                        out=vaug[:, tile * 260:tile * 260 + 260].rearrange(
                            "p (h c) -> p h c", h=4)[:, :, 0:64],
                        in_=pA[:, bank * 512:bank * 512 + 256].rearrange("p (h c) -> p h c", h=4),
                        func=AF.Copy)], reads=[pa[bank]], writes=[rvaug])

            for kb in range(NQT if part >= 3 else 0):
                qn = min(384, T - kb * 128)
                npc = qn // 128
                slot = kb % 4
                mms = []
                for h in range(4):
                    for hf, src_q, src_k in ((0, qTA, kTA), (1, qTB, kTB)):
                        mms.append(lambda eng, h=h, hf=hf, src_q=src_q, src_k=src_k, kb=kb, qn=qn: eng.matmul(
                            pA[:, h * 512:h * 512 + qn],
                            src_k[32 * h:32 * h + 32, kb * 128:kb * 128 + 128],
                            src_q[32 * h:32 * h + 32, kb * 128:kb * 128 + qn],
                            start=(hf == 0), stop=(hf == 1), tile_position=(32 * h, 0)))
                p.g("t", mms, reads=[rqT, rkT], writes=[pa[0], pa[1], pa[2], pa[3]])
                rpt = res(f"m_pt{slot}")
                p.g("s", [lambda eng, slot=slot, qn=qn: eng.activation(
                    out=ptile[:, slot * 1536:slot * 1536 + 1536].rearrange(
                        "p (h c) -> p h c", h=4)[:, :, 0:qn],
                    in_=pA[:, 0:2048].rearrange("p (h c) -> p h c", h=4)[:, :, 0:qn],
                    func=AF.Exp, scale=8.0)],
                    reads=[pa[0], pa[1], pa[2], pa[3]], writes=[rpt])
                mask_ops = [lambda eng, slot=slot: eng.tensor_tensor(
                    out=ptile[:, slot * 1536:slot * 1536 + 1536].rearrange(
                        "p (h c) -> p h c", h=4)[:, :, 0:128],
                    in0=ptile[:, slot * 1536:slot * 1536 + 1536].rearrange(
                        "p (h c) -> p h c", h=4)[:, :, 0:128],
                    in1=mown.rearrange("p (u c) -> p u c", u=1).broadcast_to([128, 4, 128]),
                    op=ALU.mult)]
                if npc == 3:
                    mask_ops.append(lambda eng, slot=slot: eng.tensor_tensor(
                        out=ptile[:, slot * 1536:slot * 1536 + 1536].rearrange(
                            "p (h c) -> p h c", h=4)[:, :, 256:384],
                        in0=ptile[:, slot * 1536:slot * 1536 + 1536].rearrange(
                            "p (h c) -> p h c", h=4)[:, :, 256:384],
                        in1=mold.rearrange("p (u c) -> p u c", u=1).broadcast_to([128, 4, 128]),
                        op=ALU.mult))
                p.g("v", mask_ops, reads=[res("c_masks")], writes=[rpt])
                t = kb
                kbs = list(range(max(0, t - 2), t + 1))
                mms = []
                for h in range(4):
                    for i_, kk in enumerate(kbs):
                        j = t - kk
                        mms.append(lambda eng, h=h, kk=kk, j=j, t=t,
                                   first=(i_ == 0), last=(i_ == len(kbs) - 1): eng.matmul(
                            pB[0:65, h * 512 + 128 * (t % 4): h * 512 + 128 * (t % 4) + 128],
                            vaug[:, kk * 260 + 65 * h: kk * 260 + 65 * h + 65],
                            ptile[:, (kk % 4) * 1536 + 384 * h + 128 * j:
                                  (kk % 4) * 1536 + 384 * h + 128 * j + 128],
                            start=first, stop=last))
                rpts = [res(f"m_pt{kk % 4}") for kk in kbs]
                p.g("t", mms, reads=rpts + [rvaug], writes=[pb[0], pb[1], pb[2], pb[3]])
                p.g("s", [
                    lambda eng, t=t, quad=quad: eng.activation(
                        out=att[0:64, :].rearrange("p (c n) -> p c n", c=NCH)[
                            :, 2 * quad:2 * quad + 2, t * 128:t * 128 + 128],
                        in_=pB[0:64, :].rearrange("p (a b n) -> p a b n", a=2, b=2)[
                            :, :, 0, 128 * (t % 4):128 * (t % 4) + 128],
                        func=AF.Copy),
                    lambda eng, t=t, quad=quad: eng.activation(
                        out=att[64:128, :].rearrange("p (c n) -> p c n", c=NCH)[
                            :, 2 * quad:2 * quad + 2, t * 128:t * 128 + 128],
                        in_=pB[0:64, :].rearrange("p (a b n) -> p a b n", a=2, b=2)[
                            :, :, 1, 128 * (t % 4):128 * (t % 4) + 128],
                        func=AF.Copy),
                ], reads=[pb[0], pb[1], pb[2], pb[3]], writes=[ratt])
                rsum = res("m_asums")
                p.g("v", [lambda eng, t=t: eng.reciprocal(
                    out=asums[0:1, :].rearrange("p (h n) -> p h n", h=4)[
                        :, :, 128 * (t % 4):128 * (t % 4) + 128],
                    in_=pB[64:65, :].rearrange("p (h n) -> p h n", h=4)[
                        :, :, 128 * (t % 4):128 * (t % 4) + 128])],
                    reads=[pb[0], pb[1], pb[2], pb[3]], writes=[rsum])
                p.g("t", [lambda eng, t=t, h=h: eng.matmul(
                    pA[64 * (h % 2):64 * (h % 2) + 64,
                       (h // 2) * 512 + 384:(h // 2) * 512 + 512],
                    ones1f[0:1, 0:64],
                    asums[0:1, h * 512 + 128 * (t % 4): h * 512 + 128 * (t % 4) + 128],
                    start=True, stop=True, tile_position=(0, 64 * (h % 2)))
                    for h in range(4)],
                    reads=[rsum, res("c_ones")], writes=[pa[0], pa[1]])
                if debug != "mattu":
                    p.g("v", [lambda eng, t=t, quad=quad, c=c: eng.tensor_tensor(
                        out=att[:, (2 * quad + c) * T + t * 128:(2 * quad + c) * T + t * 128 + 128],
                        in0=att[:, (2 * quad + c) * T + t * 128:(2 * quad + c) * T + t * 128 + 128],
                        in1=pA[:, c * 512 + 384:c * 512 + 512], op=ALU.mult)
                        for c in range(2)], reads=[pa[0], pa[1], ratt], writes=[ratt])

            if debug == "masums" and s == 0:
                rdmp = res("m_fscr0")
                for hx in range(2):
                    p.g("v", [lambda eng, hx=hx: eng.tensor_copy(
                        out=fscr[0:1, 0:1024], in_=asums[0:1, hx * 1024:(hx + 1) * 1024])],
                        reads=[res("m_asums")], writes=[rdmp])
                    p.dma("g", out_t[s, quad, hx:hx + 1, 0:1024], fscr[0:1, 0:1024], reads=[rdmp])
            if debug in ("mq", "mk", "mqraw", "mv") and s == 0:
                rdmp = res("m_fscr0")
                if debug in ("mq", "mk", "mqraw"):
                    srcs = (qTA, qTB) if debug in ("mq", "mqraw") else (kTA, kTB)
                    for i2, sb in enumerate(srcs):
                        for tg2 in range(NTG):
                            p.g("v", [lambda eng, sb=sb, tg2=tg2: eng.tensor_copy(
                                out=fscr[:, 0:TG], in_=sb[:, tg2 * TG:(tg2 + 1) * TG])],
                                reads=[rqT, rkT], writes=[rdmp])
                            p.dma("g", out_t[s, 2 * quad + i2, :, tg2 * TG:(tg2 + 1) * TG],
                                  fscr[:, 0:TG], reads=[rdmp])
                else:
                    for tt in range(NQT):
                        p.g("v", [lambda eng, tt=tt: eng.tensor_copy(
                            out=fscr[:, 0:260], in_=vaug[:, tt * 260:(tt + 1) * 260])],
                            reads=[rvaug], writes=[rdmp])
                        p.dma("g", out_t[s, 4 + quad, :, tt * 260 % T:(tt * 260 % T) + 260]
                              if False else out_t[s, quad, :, (tt % 7) * 260:(tt % 7) * 260 + 260],
                              fscr[:, 0:260], reads=[rdmp])

        if debug in ("matt", "mattu") and s == 0:
            rdmp = res("m_fscr0")
            for ch in range(NCH):
                for tg in range(NTG):
                    p.g("v", [lambda eng, ch=ch, tg=tg: eng.tensor_copy(
                        out=fscr[:, 0:TG], in_=att[:, ch * T + tg * TG:ch * T + tg * TG + TG])],
                        reads=[ratt], writes=[rdmp])
                    p.dma("g", out_t[s, ch, :, tg * TG:(tg + 1) * TG], fscr[:, 0:TG],
                          reads=[rdmp])

        for oc in range(NCH if part >= 4 else 0):
            wo_r = res(f"mwo{oc % 2}")
            slo = wring[:, 6144 + (oc % 2) * 1024: 6144 + (oc % 2) * 1024 + 1024]
            p.dma("q", slo.rearrange("p (ic c) -> p ic c", c=128),
                  wo_bf[L, :, oc * 128:(oc + 1) * 128].rearrange("(ic p) c -> p ic c", p=128),
                  writes=[wo_r], extra=[("g", pc_done[f"wo{L}"])])
            for tg in range(NTG):
                bank = tg % 2
                p.g("t", [lambda eng, ic=ic, tg=tg, bank=bank, slo=slo: eng.matmul(
                    pbS(bank), slo[:, ic * 128:(ic + 1) * 128],
                    att[:, ic * T + tg * TG: ic * T + tg * TG + TG],
                    start=(ic == 0), stop=(ic == NCH - 1)) for ic in range(NCH)],
                    reads=[wo_r, ratt], writes=[pb[bank]])
                ry = res(f"m_yld{bank}")
                p.dma("q", yres[:, bank * TG:(bank + 1) * TG],
                      Yin[s, oc, :, tg * TG:(tg + 1) * TG], writes=[ry])
                rfs = res(f"m_fscr{bank}")
                p.g("v", [lambda eng, bank=bank: eng.tensor_tensor(
                    out=fscr[:, bank * TG:(bank + 1) * TG], in0=pbS(bank),
                    in1=yres[:, bank * TG:(bank + 1) * TG], op=ALU.add)],
                    reads=[pb[bank], ry], writes=[rfs])
                p.dma("g", Yout[s, oc, :, tg * TG:(tg + 1) * TG],
                      fscr[:, bank * TG:(bank + 1) * TG], reads=[rfs])
                if debug == f"ph{2 * L}":
                    p.dma("g", out_t[s, oc, :, tg * TG:(tg + 1) * TG],
                          fscr[:, bank * TG:(bank + 1) * TG], reads=[rfs])

    def ffn_phase(L, s, Yin, Yout, last):
        p.barrier()
        p.dma("q", w2res[:, 0:21 * 1024].rearrange("p (mc c) -> p mc c", c=1024),
              w2_bf[L, 0:2688, :].rearrange("(mc p) c -> p mc c", p=128),
              writes=[rw2res], extra=[("g", pc_done[f"w2{L}"])])
        p.dma("q", w2res[0:MCL, 21 * 1024:22 * 1024], w2_bf[L, 2688:2730, :],
              writes=[rw2res])
        for tg in range(NTG):
            lo = tg * TG
            ysl = tg % 2
            ry = res(f"f_y{ysl}")
            for ch in range(NCH):
                if lo == 0:
                    p.dma("q", yf[:, (ysl * NCH + ch) * 513 + 1:(ysl * NCH + ch) * 513 + 513],
                          Yin[s, ch, :, 0:TG], writes=[ry])
                else:
                    p.dma("q", yf[:, (ysl * NCH + ch) * 513:(ysl * NCH + ch) * 513 + 513],
                          Yin[s, ch, :, lo - 1:lo + TG], writes=[ry])
            if tg == 0:
                p.g("v", [lambda eng, ch=ch: eng.memset(
                    yf[:, ch * 513:ch * 513 + 1], 0.0) for ch in range(NCH)], writes=[ry])
            for ch in range(NCH):
                sl = ch % 2
                rc = res(f"f_c{sl}")
                p.g("v", [lambda eng, ch=ch, sl=sl, ysl=ysl: eng.tensor_copy(
                    out=f_sqbf[:, sl * TG:(sl + 1) * TG], in_=yf_sh(ysl, ch))],
                    reads=[ry], writes=[rc])
                p.g("t", [lambda eng, ch=ch, sl=sl: eng.matmul(
                    pB[0:1, 0:TG], ones_bf[:, 0:1], f_sqbf[:, sl * TG:(sl + 1) * TG],
                    start=(ch == 0), stop=(ch == NCH - 1))],
                    reads=[rc, res("c_ones")], writes=[pb[0]])
                p.g("v", [lambda eng, sl=sl: eng.tensor_tensor(
                    out=f_sqbf[:, sl * TG:(sl + 1) * TG], in0=f_sqbf[:, sl * TG:(sl + 1) * TG],
                    in1=f_sqbf[:, sl * TG:(sl + 1) * TG], op=ALU.mult)], reads=[rc], writes=[rc])
                p.g("t", [lambda eng, ch=ch, sl=sl: eng.matmul(
                    pB[0:1, 512:512 + TG], ones_bf[:, 0:1], f_sqbf[:, sl * TG:(sl + 1) * TG],
                    start=(ch == 0), stop=(ch == NCH - 1))], reads=[rc], writes=[pb[1]])
            rr = res("f_rows")
            p.g("v", [
                lambda eng: eng.tensor_scalar_mul(out=f_srows[0:1, 0:TG], in0=pB[0:1, 0:TG],
                                                  scalar1=1.0 / D),
                lambda eng: eng.tensor_scalar_mul(out=f_srows[0:1, 512:1024], in0=pB[0:1, 512:512 + TG],
                                                  scalar1=1.0 / D),
                lambda eng: eng.tensor_tensor(out=f_srows[0:1, 1024:1024 + TG], in0=f_srows[0:1, 0:TG],
                                              in1=f_srows[0:1, 0:TG], op=ALU.mult),
                lambda eng: eng.tensor_tensor(out=f_srows[0:1, 512:1024], in0=f_srows[0:1, 512:1024],
                                              in1=f_srows[0:1, 1024:1024 + TG], op=ALU.subtract),
                lambda eng: eng.tensor_scalar_add(out=f_srows[0:1, 512:1024], in0=f_srows[0:1, 512:1024],
                                                  scalar1=EPS),
            ], reads=[pb[0], pb[1]], writes=[rr])
            p.g("s", [lambda eng: eng.activation(out=f_srows[0:1, 512:1024],
                                                 in_=f_srows[0:1, 512:1024], func=AF.Sqrt)],
                reads=[rr], writes=[rr])
            p.g("v", [lambda eng: eng.reciprocal(out=f_srows[0:1, 512:1024], in_=f_srows[0:1, 512:1024]),
                      lambda eng: eng.tensor_tensor(out=f_srows[0:1, 0:TG], in0=f_srows[0:1, 0:TG],
                                                    in1=f_srows[0:1, 512:1024], op=ALU.mult)],
                reads=[rr], writes=[rr])
            p.g("t", [
                lambda eng: eng.matmul(pbS(0), ones1f, f_srows[0:1, 512:1024], start=True, stop=True),
                lambda eng: eng.matmul(pbS(1), ones1f, f_srows[0:1, 0:TG], start=True, stop=True),
            ], reads=[rr, res("c_ones")], writes=[pb[0], pb[1]])
            rxn = res(f"f_xn{ysl}")
            for ch in range(NCH):
                p.g("v", [
                    lambda eng, ch=ch, ysl=ysl: eng.tensor_tensor(
                        out=f_fscr[:, 0:TG], in0=yf_sh(ysl, ch), in1=pbS(0), op=ALU.mult),
                    lambda eng: eng.tensor_tensor(
                        out=f_fscr[:, TG:2 * TG], in0=f_fscr[:, 0:TG], in1=pbS(1), op=ALU.subtract),
                    lambda eng, ch=ch, ysl=ysl, L=L: eng.tensor_scalar(
                        out=xnbf[:, (ysl * NCH + ch) * TG:(ysl * NCH + ch) * TG + TG],
                        in0=f_fscr[:, TG:2 * TG],
                        scalar1=lng[:, L * NCH + ch:L * NCH + ch + 1],
                        scalar2=lnb[:, L * NCH + ch:L * NCH + ch + 1],
                        op0=ALU.mult, op1=ALU.add),
                ], reads=[ry, pb[0], pb[1], res("c_lng_col")],
                    writes=[rxn, res("f_fscr_sc"), res("f_ev0"), res("f_ev1")])
            if debug == "mxn" and s == 0:
                rdmp = res("f_ev0")
                for ch in range(NCH):
                    p.g("v", [lambda eng, ch=ch, ysl=ysl: eng.tensor_copy(
                        out=f_fscr[:, 0:TG],
                        in_=xnbf[:, (ysl * NCH + ch) * TG:(ysl * NCH + ch) * TG + TG])],
                        reads=[res(f"f_xn{ysl}")], writes=[rdmp])
                    p.dma("g", out_t[s, ch, :, lo:lo + TG], f_fscr[:, 0:TG], reads=[rdmp])
            for p2 in range(0, NMC, 2):
                wg = res(f"fw1{(p2 // 2) % 2}")
                gw = 256 if p2 < NMC - 2 else (128 + MCL)
                offp = ((p2 // 2) % 2) * 4096
                sla = wring[:, offp: offp + 8 * gw]
                slg = wring[:, offp + 8 * gw: offp + 2 * 8 * gw]
                p.dma("q", sla.rearrange("p (kc c) -> p kc c", c=gw),
                      w1_bf[L, :, p2 * 128:p2 * 128 + gw].rearrange("(kc p) c -> p kc c", p=128),
                      writes=[wg], extra=[("g", pc_done[f"w1{L}"])])
                p.dma("q", slg.rearrange("p (kc c) -> p kc c", c=gw),
                      w1_bf[L, :, DI + p2 * 128:DI + p2 * 128 + gw].rearrange("(kc p) c -> p kc c", p=128),
                      writes=[wg])
                for pp in (p2, p2 + 1):
                    pbank = (pp % 2) * 2
                    coff = (pp - p2) * 128
                    mw = 128 if pp < NMC - 1 else MCL
                    p.g("t", [lambda eng, kc=kc, pbank=pbank, coff=coff, mw=mw, sla=sla, gw=gw, ysl=ysl: eng.matmul(
                        pA[0:mw, pbank * 512:pbank * 512 + TG],
                        sla[:, kc * gw + coff: kc * gw + coff + mw],
                        xnbf[:, (ysl * NCH + kc) * TG:(ysl * NCH + kc) * TG + TG],
                        start=(kc == 0), stop=(kc == NCH - 1)) for kc in range(NCH)],
                        reads=[wg, rxn], writes=[pa[pbank]])
                    p.g("t", [lambda eng, kc=kc, pbank=pbank, coff=coff, mw=mw, slg=slg, gw=gw, ysl=ysl: eng.matmul(
                        pA[0:mw, (pbank + 1) * 512:(pbank + 1) * 512 + TG],
                        slg[:, kc * gw + coff: kc * gw + coff + mw],
                        xnbf[:, (ysl * NCH + kc) * TG:(ysl * NCH + kc) * TG + TG],
                        start=(kc == 0), stop=(kc == NCH - 1)) for kc in range(NCH)],
                        reads=[wg, rxn], writes=[pa[pbank + 1]])
                    rg = res(f"f_g{pp % 2}")
                    p.g("s", [lambda eng, pbank=pbank, mw=mw, pp=pp, L=L: eng.activation(
                        out=gelu_t[0:mw, (pp % 2) * TG:(pp % 2) * TG + TG],
                        in_=pA[0:mw, (pbank + 1) * 512:(pbank + 1) * 512 + TG],
                        func=AF.Gelu, bias=b1g[0:mw, L * NMC + pp:L * NMC + pp + 1])],
                        reads=[pa[pbank + 1]], writes=[rg])
                    rm = res("f_mT")
                    p.g("v", [lambda eng, pbank=pbank, mw=mw, pp=pp, L=L: eng.scalar_tensor_tensor(
                        out=mT[0:mw, pp * TG:(pp + 1) * TG],
                        in0=pA[0:mw, pbank * 512:pbank * 512 + TG],
                        scalar=b1a[0:mw, L * NMC + pp:L * NMC + pp + 1],
                        in1=gelu_t[0:mw, (pp % 2) * TG:(pp % 2) * TG + TG],
                        op0=ALU.add, op1=ALU.mult)],
                        reads=[pa[pbank], rg, res("c_b1a_col")], writes=[rm])
            if debug == "mmid" and s == 0 and tg == 0:
                rdmp = res("f_ev0")
                for c in range(NMC):
                    p.g("v", [lambda eng, c=c: eng.tensor_copy(
                        out=f_fscr[:, 0:TG], in_=mT[:, c * TG:(c + 1) * TG])],
                        reads=[res("f_mT")], writes=[rdmp])
                    p.dma("g", out_t[s, c % 8, :, (c // 8) * TG:(c // 8) * TG + TG],
                          f_fscr[:, 0:TG], reads=[rdmp])
            for ocp in range(4):
                mms = []
                for oc in (2 * ocp, 2 * ocp + 1):
                    for mc in range(NMC):
                        mw = 128 if mc < NMC - 1 else MCL
                        mms.append(lambda eng, oc=oc, mc=mc, mw=mw: eng.matmul(
                            pB[:, (2 + (oc % 2)) * 512:(2 + (oc % 2)) * 512 + TG],
                            w2res[0:mw, mc * 1024 + oc * 128: mc * 1024 + oc * 128 + 128],
                            mT[0:mw, mc * TG:(mc + 1) * TG],
                            start=(mc == 0), stop=(mc == NMC - 1)))
                p.g("t", mms, reads=[rw2res, res("f_mT")], writes=[pb[2], pb[3]])
                for oc in (2 * ocp, 2 * ocp + 1):
                    rfs = res(f"f_ev{oc % 2}")
                    p.g("v", [lambda eng, oc=oc, ysl=ysl, L=L: eng.scalar_tensor_tensor(
                        out=f_fscr[:, (oc % 2) * TG:(oc % 2) * TG + TG],
                        in0=pB[:, (2 + (oc % 2)) * 512:(2 + (oc % 2)) * 512 + TG],
                        scalar=b2c[:, L * NCH + oc:L * NCH + oc + 1],
                        in1=yf_plain(ysl, oc), op0=ALU.add, op1=ALU.add)],
                        reads=[pb[2 + (oc % 2)], ry, res("c_b2_col")],
                        writes=[rfs, res("f_fscr_sc")])
                    if not last:
                        p.dma("g", Yout[s, oc, :, lo:lo + TG],
                              f_fscr[:, (oc % 2) * TG:(oc % 2) * TG + TG], reads=[rfs])
                        if debug == f"ph{2 * L + 1}":
                            p.dma("g", out_t[s, oc, :, lo:lo + TG],
                                  f_fscr[:, (oc % 2) * TG:(oc % 2) * TG + TG], reads=[rfs])
                    else:
                        rx1 = res("f_x1")
                        p.dma("q", f_fscr[:, 2 * TG:3 * TG], Y1[s, oc, :, lo:lo + TG],
                              writes=[rx1])
                        p.g("v", [lambda eng, oc=oc: eng.tensor_tensor(
                            out=f_fscr[:, (oc % 2) * TG:(oc % 2) * TG + TG],
                            in0=f_fscr[:, (oc % 2) * TG:(oc % 2) * TG + TG],
                            in1=f_fscr[:, 2 * TG:3 * TG], op=ALU.add)],
                            reads=[rx1, rfs], writes=[rfs])
                        p.dma("g", out_t[s, oc, :, lo:lo + TG],
                              f_fscr[:, (oc % 2) * TG:(oc % 2) * TG + TG], reads=[rfs])

    maxph = 99
    if debug is not None and str(debug).startswith("ph"):
        maxph = int(str(debug)[2:])
    if debug in ("mq", "mk", "mv", "matt", "mqraw", "mattu", "masums"):
        maxph = 0
    if debug in ("mxn", "mmid"):
        maxph = 1
    if debug is not None and str(debug).startswith("a1"):
        part = {"resid": 1, "qkv": 2, "core": 3, "full": 4}[str(debug)[2:]]
        attn_phase(0, 0, Ya, Yb, part)
    if debug is not None and str(debug).startswith("aN"):
        nph = int(str(debug)[2:])
        for s in range(nph):
            attn_phase(0, s, Ya, Yb)
    if debug == "f1":
        attn_phase(0, 0, Ya, Yb)
        ffn_phase(0, 0, Yb, Ya, last=False)
    if debug is None or str(debug).startswith("ph") or debug in ("mq", "mk", "mv", "matt", "mqraw", "mattu", "masums", "mxn", "mmid"):
        for L in range(DEPTH):
            if 2 * L > maxph:
                break
            for s in range(SPC):
                attn_phase(L, s, Ya, Yb)
            if 2 * L + 1 > maxph:
                break
            for s in range(SPC):
                ffn_phase(L, s, Yb, Ya, last=(L == DEPTH - 1))

    sems = {}
    _es = ExitStack()
    for e in ENGS:
        sems[e] = _es.enter_context(nc.semaphore(f"sem_{e}"))
    p.g("v", [lambda eng: eng.memset(ones1f, 1.0)],
        extra=[("g", p.n["g"]), ("q", p.n["q"])])
    p.finalize(sems)
    nc._kernel_es = _es
    return nc


def kernel(**inputs):
    consts = _host_consts()
    prep = _host_prep(inputs)
    x = np.asarray(inputs["L0_attn_context"], np.float32)
    nc = build(_DEBUG)
    in_maps = []
    for core in range(NCORES):
        xs = x[core * SPC:(core + 1) * SPC]
        xt = np.ascontiguousarray(xs.transpose(0, 2, 1)).reshape(SPC, NCH, 128, T)
        m = {"x": xt, "wqkv": prep["wqkv"], "wo": prep["wo"], "w1": prep["w1"],
             "w2": prep["w2"], "wa": prep["wa"],
             "ba_col": prep["ba_col"], "bav_bc": prep["bav_bc"],
             "lng_col": prep["lng_col"], "lnb_col": prep["lnb_col"],
             "b1a_col": prep["b1a_col"], "b1g_col": prep["b1g_col"],
             "b2_col": prep["b2_col"], "sel01": prep["sel01"],
             "qs_col": prep["qs_col"], "ks_col": prep["ks_col"],
             "cs": consts["cs"], "sn": consts["sn"], "ci": consts["ci"], "si": consts["si"],
             "mask_own": consts["mask_own"], "mask_old": consts["mask_old"]}
        in_maps.append(m)
    res_ = run_bass_kernel_spmd(nc, in_maps, core_ids=list(range(NCORES)))
    outs = []
    for core in range(NCORES):
        o = res_.results[core]["out"].reshape(SPC, D, T)
        outs.append(o.transpose(0, 2, 1))
    return np.ascontiguousarray(np.concatenate(outs, axis=0))
